# version 01 (final)
# baseline (speedup 1.0000x reference)
import numpy as np
import ml_dtypes
import concourse.bass as bass
import concourse.mybir as mybir
from concourse.bass_utils import run_bass_kernel_spmd
from contextlib import ExitStack

F32 = mybir.dt.float32
F16 = mybir.dt.float16
BF16 = mybir.dt.bfloat16
AF = mybir.ActivationFunctionType
ALU = mybir.AluOpType

B, T, HID = 4, 2048, 2048
NK, NV, DK, DV = 16, 32, 128, 128
KEY_DIM, VALUE_DIM = NK * DK, NV * DV
INTER = 8192
NCTRL = 130156
NSAMP = INTER * HID
C = 128
NCH = T // C
FT = HID // 128
IT = INTER // 128
TB = 512
NTB = T // TB
TH = T // 2

ENGINES = ("sync", "vector", "scalar", "gpsimd", "tensor")


class _Op:
    __slots__ = ("meth", "args", "kwargs", "incs")
    def __init__(self, meth, args, kwargs):
        self.meth, self.args, self.kwargs = meth, args, kwargs
        self.incs = []


class _EngRec:
    def __init__(self, prog, name):
        self._prog, self._name = prog, name

    def __getattr__(self, meth):
        def call(*args, **kwargs):
            op = _Op(meth, args, kwargs)
            self._prog.ops[self._name].append(op)
            return op
        return call

    def wait(self, sem_name, val=None):
        p = self._prog
        v = p.cnt[sem_name] if val is None else val
        if p.last_wait[self._name].get(sem_name) == v:
            return None
        p.last_wait[self._name][sem_name] = v
        op = _Op("wait_ge", ("__SEM__", sem_name, v), {})
        p.ops[self._name].append(op)
        return op

    def fence(self, sem_name=None):
        op = _Op("drain", (), {})
        self._prog.ops[self._name].append(op)

    def inc(self, sem_name, amt=1):
        p = self._prog
        ops = p.ops[self._name]
        assert ops and ops[-1].meth not in ("wait_ge", "drain"), f"no producing op on {self._name}"
        assert not ops[-1].incs, f"op already has an inc on {self._name}"
        p.cnt[sem_name] += amt
        ops[-1].incs.append((sem_name, amt, p.cnt[sem_name]))
        return p.cnt[sem_name]


class Prog:
    def __init__(self):
        self.ops = {e: [] for e in ENGINES}
        self.cnt = {}
        self.sem_names = []
        self.last_wait = {e: {} for e in ENGINES}
        for e in ENGINES:
            setattr(self, e, _EngRec(self, e))
        self.sem("bar")

    def sem(self, name):
        if name not in self.cnt:
            self.cnt[name] = 0
            self.sem_names.append(name)
        return name

    def barrier(self):
        DMA_OPS = ("dma_start", "dma_start_transpose", "indirect_dma_start")
        producers = []
        for e in ENGINES:
            ops = self.ops[e]
            if not ops:
                continue
            if ops[-1].meth in ("wait_ge", "drain"):
                nopop = _Op("nop", (), {})
                ops.append(nopop)
                self.cnt["bar"] += 1
                nopop.incs.append(("bar", 1, self.cnt["bar"]))
                producers.append(("bar", self.cnt["bar"]))
                continue
            last = ops[-1]
            if last.incs:
                sem, amt, cum = last.incs[0]
                producers.append((sem, cum))
            else:
                amt = 16 if last.meth in DMA_OPS else 1
                self.cnt["bar"] += amt
                last.incs.append(("bar", amt, self.cnt["bar"]))
                producers.append(("bar", self.cnt["bar"]))
        agg = {}
        for sem, val in producers:
            agg[sem] = max(agg.get(sem, 0), val)
        for e in ENGINES:
            if self.ops[e]:
                for sem, val in agg.items():
                    getattr(self, e).wait(sem, val)

    def emit(self, nc, reg_setup=None):
        stack = ExitStack()
        sems = {n: stack.enter_context(nc.semaphore(n[:28])) for n in self.sem_names}
        blk = stack.enter_context(nc.Block())

        def make_runner(ename):
            def runner(eng_obj):
                regs = {}
                if reg_setup is not None:
                    regs = reg_setup(ename, eng_obj) or {}

                def resolve(a):
                    if callable(a) and getattr(a, "_lazy_ap", False):
                        return a(regs)
                    return a
                for op in self.ops[ename]:
                    if op.meth == "wait_ge" and op.args and op.args[0] == "__SEM__":
                        eng_obj.wait_ge(sems[op.args[1]], op.args[2])
                        continue
                    args = [resolve(a) for a in op.args]
                    kwargs = {k: resolve(v) for k, v in op.kwargs.items()}
                    ins = getattr(eng_obj, op.meth)(*args, **kwargs)
                    for sem_name, amt, _cum in op.incs:
                        ins.then_inc(sems[sem_name], amt)
            return runner

        for e in ENGINES:
            if not self.ops[e]:
                continue
            getattr(blk, e)(make_runner(e))
        stack.close()


def lazy(fn):
    fn._lazy_ap = True
    return fn


_HOST = {}

def _grid():
    if "grid" not in _HOST:
        t = np.linspace(0.0, NCTRL - 1.0, NSAMP, dtype=np.float64)
        i = np.clip(np.floor(t).astype(np.int64), 0, NCTRL - 2)
        f = (t - i).astype(np.float32)
        _HOST["grid"] = (i.astype(np.int64), f)
    return _HOST["grid"]

def _phi_w0(w_in):
    key = ("phi", w_in)
    if key not in _HOST:
        i_arr, f = _grid()
        w_out = NSAMP // w_in
        na, nb = w_out // 128, w_in // 128
        f2 = f * f
        f3 = f2 * f
        cks = np.stack([
            0.5 * (-f + 2 * f2 - f3), 0.5 * (2 - 5 * f2 + 3 * f3),
            0.5 * (f + 4 * f2 - 3 * f3), 0.5 * (-f2 + f3)], 0).astype(np.float16)
        i_t = i_arr.reshape(na, 128, nb, 128).transpose(0, 2, 1, 3)
        ck_t = cks.reshape(4, na, 128, nb, 128).transpose(1, 3, 0, 2, 4)
        w0 = i_t[:, :, :, 0] - 1
        o = i_t - w0[:, :, :, None] - 1
        phi = np.zeros((na, nb, 5, 128, 128), np.float16)
        for kk in range(4):
            for oo in (0, 1):
                np.copyto(phi[:, :, kk + oo], ck_t[:, :, kk], where=(o == oo))
        phi = np.ascontiguousarray(phi.reshape(na * nb, 5, 128, 128).transpose(0, 2, 1, 3).reshape(na * nb, 128, 5 * 128))
        _HOST[key] = (phi, w0)
    return _HOST[key]

def _cpwin(cp, w_in):
    _, w0 = _phi_w0(w_in)
    idx = np.clip(w0[:, :, :, None] + np.arange(8)[None, None, None, :], 0, NCTRL - 1)
    na, nb = w0.shape[0], w0.shape[1]
    return np.ascontiguousarray(cp[idx].astype(np.float32).transpose(0, 2, 1, 3).reshape(na, 128, nb * 8))

def _tile_w(w):
    n_in, n_out = w.shape
    a = w.reshape(n_in // 128, 128, n_out // 128, 128)
    return np.ascontiguousarray(a.transpose(2, 0, 1, 3))


def build(debug_taps=(), upto=99, tail_upto=99):
    nc = bass.Bass("TRN2", target_bir_lowering=False, dynamic_dma_scratch_size=16384)
    P = Prog()
    st = ExitStack()
    E = st.enter_context
    dram = {}
    ctx = {}

    def din(name, shape, dtype=F32):
        h = nc.dram_tensor(name, shape, dtype, kind="ExternalInput")
        dram[name] = h
        return h.ap()

    def dscr(name, shape, dtype):
        kind = "ExternalOutput" if name in debug_taps else "Internal"
        h = nc.dram_tensor(name, shape, dtype, kind=kind)
        dram[name] = h
        return h.ap()

    d = ctx
    d["x_fm"] = din("x_fm", [HID, T])
    d["wq_t"] = din("wq_t", [96, FT, 128, 128])
    d["wo_t"] = din("wo_t", [FT, 32, 128, 128])
    d["wba_t"] = din("wba_t", [FT, 128, 64])
    d["phi_gu"] = din("phi_gu", [IT * FT, 128, 5 * 128], F16)
    d["phi_dn"] = din("phi_dn", [FT * IT, 128, 5 * 128], F16)
    d["cpw_g"] = din("cpw_g", [IT, 128, FT * 8])
    d["cpw_u"] = din("cpw_u", [IT, 128, FT * 8])
    d["cpw_d"] = din("cpw_d", [FT, 128, IT * 8])
    d["convw_in"] = din("convw_in", [128, 256])
    d["smalls_in"] = din("smalls_in", [128, 8])
    d["halfsel"] = din("halfsel", [1, 2], mybir.dt.int32)
    d["y_fm"] = nc.dram_tensor("y_fm", [HID, TH], F32, kind="ExternalOutput").ap()
    dram["y_fm"] = None

    d["xn1_d"] = dscr("xn1_d", [FT, 128, T], BF16)
    d["wg_d"] = dscr("wg_d", [IT, FT, 128, 128], BF16)
    d["wu_d"] = dscr("wu_d", [IT, FT, 128, 128], BF16)
    d["wd_d"] = dscr("wd_d", [FT, IT, 128, 128], BF16)
    d["a_d"] = dscr("a_d", [IT, 128, T], BF16)
    d["x1_d"] = dscr("x1_d", [FT, 128, T], F32)
    d["xn2_d"] = dscr("xn2_d", [FT, 128, T], BF16)
    d["mixed_d"] = dscr("mixed_d", [64, 128, T], BF16)
    d["z_d"] = dscr("z_d", [NV, 128, T], BF16)
    d["qn_d"] = dscr("qn_d", [NK, 128, T], BF16)
    d["kn_d"] = dscr("kn_d", [NK, 128, T], BF16)
    d["vc_d"] = dscr("vc_d", [NV, 128, T], BF16)
    d["gated_d"] = dscr("gated_d", [NV, 128, T], BF16)
    d["x2_d"] = dscr("x2_d", [FT, 128, TH], F32)
    d["gates_d"] = dscr("gates_d", [128, T], F32)
    d["rows_d"] = dscr("rows_d", [2, 32, T], F32)
    d["Arow_d"] = dscr("Arow_d", [32, T], BF16)
    d["planes_d"] = dscr("planes_d", [128, T], F32)

    sy, ve, sc, gp, pe = P.sync, P.vector, P.scalar, P.gpsimd, P.tensor
    for s in ["ld", "vd", "ad", "pd", "gd", "sd", "wr", "rc", "ld2", "pd2",
              "vd2", "ad2", "gd2", "dn"]:
        P.sem(s)

    onescol_b = E(nc.sbuf_tensor("onescol_b", [128, 1], BF16))
    onesrow_b = E(nc.sbuf_tensor("onesrow_b", [1, 128], BF16))
    onesrow_f = E(nc.sbuf_tensor("onesrow_f", [1, 128], F32))
    ident_b = E(nc.sbuf_tensor("ident_b", [128, 128], BF16))
    maskLn = E(nc.sbuf_tensor("maskLn", [128, 128], F32))
    maskUn = E(nc.sbuf_tensor("maskUn", [128, 128], F32))
    maskUi = E(nc.sbuf_tensor("maskUi", [128, 128], F32))
    smalls = E(nc.sbuf_tensor("smalls", [128, 8], F32))
    convw = E(nc.sbuf_tensor("convw", [128, 256], F32))
    d.update(onescol_b=onescol_b, onesrow_b=onesrow_b, onesrow_f=onesrow_f,
             ident_b=ident_b, maskLn=maskLn, maskUn=maskUn, maskUi=maskUi,
             smalls=smalls, convw=convw)

    Apl = E(nc.sbuf_tensor("Apl", [32, T], BF16))
    colpl = E(nc.sbuf_tensor("colpl", [128, NCH * 6 * 32], F32))
    identf = E(nc.sbuf_tensor("identf", [128, 128], F32))
    d.update(Apl=Apl, colpl=colpl, identf=identf)
    xstk = ExitStack()
    EX = xstk.enter_context
    xres = [EX(nc.sbuf_tensor(f"xres{i}", [128, T], BF16)) for i in range(FT)]
    d["xstk"] = xstk
    lnstk = ExitStack()
    EL = lnstk.enter_context
    fw = [EL(nc.sbuf_tensor(f"fw{i}", [128, TB], F32)) for i in range(FT)]
    bw = [EL(nc.sbuf_tensor(f"bw{i}", [128, TB], BF16)) for i in range(FT)]
    bw2 = [EL(nc.sbuf_tensor(f"bw2{i}", [128, TB], BF16)) for i in range(FT)]
    rowf = EL(nc.sbuf_tensor("rowf", [1, 4 * TB], F32))
    bcf = EL(nc.sbuf_tensor("bcf", [128, 2 * TB], F32))
    d.update(xres=xres, fw=fw, bw=bw, bw2=bw2, rowf=rowf, bcf=bcf, lnstk=lnstk)

    ps = [E(nc.psum_tensor(f"ps{i}", [128, 512], F32)) for i in range(8)]
    d["ps"] = ps

    sy.dma_start(out=smalls[:], in_=d["smalls_in"])
    sy.inc("ld", 16)
    sy.dma_start(out=convw[:], in_=d["convw_in"])
    sy.inc("ld", 16)
    ve.memset(onescol_b[:], 1.0)
    ve.memset(onesrow_b[:], 1.0)
    ve.memset(onesrow_f[:], 1.0)
    ve.memset(ident_b[:], 0.0)
    ve.memset(maskLn[:], 0.0)
    ve.memset(maskUn[:], 0.0)
    ve.memset(maskUi[:], 0.0)
    ve.inc("vd")
    gp.wait("vd")
    gp.affine_select(out=ident_b[:], in_=ident_b[:], compare_op=ALU.not_equal,
                     fill=1.0, base=0, pattern=[[-1, 128]], channel_multiplier=1)
    gp.affine_select(out=maskLn[:], in_=maskLn[:], compare_op=ALU.is_ge,
                     fill=-1.0, base=0, pattern=[[1, 128]], channel_multiplier=-1)
    gp.affine_select(out=maskUn[:], in_=maskUn[:], compare_op=ALU.is_ge,
                     fill=-1.0, base=0, pattern=[[-1, 128]], channel_multiplier=1)
    gp.affine_select(out=maskUi[:], in_=maskUi[:], compare_op=ALU.is_gt,
                     fill=1.0, base=0, pattern=[[-1, 128]], channel_multiplier=1)
    ve.wait("ld")
    ve.engine_nop()
    P.barrier()

    def ln_block(src_get, t0, dst_dram, dst_res, eps_ap=None):
        if eps_ap is None:
            eps_ap = smalls[0:1, 3:4]
        """src_get(i) -> f32 [128, TB] AP for feature tile i at tokens t0:t0+TB."""
        sy.wait("vd"); sy.wait("gd"); sy.wait("ad"); sy.wait("pd")
        for i in range(FT):
            sy.dma_start(out=fw[i][:], in_=src_get(i))
            sy.inc("ld", 16)
        ve.wait("ld"); ve.wait("sd")
        for i in range(FT):
            ve.tensor_copy(bw[i][:], fw[i][:])
        ve.inc("vd")
        sc.wait("ld"); sc.wait("pd")
        for i in range(FT):
            sc.activation(bw2[i][:], fw[i][:], AF.Square)
        sc.inc("ad")
        pe.wait("vd"); pe.wait("ad")
        for i in range(FT):
            pe.matmul(ps[0][0:1, 0:TB], onescol_b[:], bw[i][:], start=(i == 0), stop=(i == FT - 1))
        for i in range(FT):
            pe.matmul(ps[1][0:1, 0:TB], onescol_b[:], bw2[i][:], start=(i == 0), stop=(i == FT - 1))
        pe.inc("pd")
        ve.wait("pd")
        ve.tensor_scalar(rowf[0:1, 0:TB], ps[0][0:1, 0:TB], 1.0 / HID, None, ALU.mult)
        ve.tensor_scalar(rowf[0:1, TB:2 * TB], ps[1][0:1, 0:TB], 1.0 / HID, None, ALU.mult)
        ve.fence()
        ve.tensor_mul(rowf[0:1, 2 * TB:3 * TB], rowf[0:1, 0:TB], rowf[0:1, 0:TB])
        ve.fence()
        ve.tensor_sub(rowf[0:1, 3 * TB:4 * TB], rowf[0:1, TB:2 * TB], rowf[0:1, 2 * TB:3 * TB])
        ve.inc("vd")
        sc.wait("vd")
        sc.activation(rowf[0:1, 2 * TB:3 * TB], rowf[0:1, 3 * TB:4 * TB], AF.Ln, bias=eps_ap)
        sc.fence()
        sc.activation(rowf[0:1, TB:2 * TB], rowf[0:1, 2 * TB:3 * TB], AF.Exp, scale=-0.5)
        sc.inc("ad")
        pe.wait("ad")
        pe.matmul(ps[2][:, 0:TB], onesrow_f[:], rowf[0:1, 0:TB], start=True, stop=True)
        pe.matmul(ps[3][:, 0:TB], onesrow_f[:], rowf[0:1, TB:2 * TB], start=True, stop=True)
        pe.inc("pd")
        ve.wait("pd")
        ve.tensor_copy(bcf[:, 0:TB], ps[2][:, 0:TB])
        ve.tensor_copy(bcf[:, TB:2 * TB], ps[3][:, 0:TB])
        ve.inc("vd")
        gp.wait("vd"); gp.wait("sd")
        for i in range(FT):
            eng = ve if (i % 2 == 0) else gp
            eng.tensor_sub(fw[i][:], fw[i][:], bcf[:, 0:TB])
        ve.fence(); gp.fence()
        for i in range(FT):
            eng = ve if (i % 2 == 0) else gp
            eng.tensor_mul(bw[i][:], fw[i][:], bcf[:, TB:2 * TB])
        if dst_res is not None:
            ve.fence(); gp.fence()
            for i in range(FT):
                eng = ve if (i % 2 == 0) else gp
                eng.tensor_copy(dst_res[i][:, t0:t0 + TB], bw[i][:])
        ve.inc("vd")
        gp.inc("gd")
        if dst_dram is not None:
            sc.wait("vd"); sc.wait("gd")
            for i in range(FT):
                sc.dma_start(out=dst_dram[i, :, t0:t0 + TB], in_=bw[i][:])
                sc.inc("sd", 16)
    d["ln_block"] = ln_block

    if upto >= 1:
        for tb in range(NTB):
            t0 = tb * TB
            ln_block(lambda i, t0=t0: d["x_fm"][128 * i:128 * (i + 1), t0:t0 + TB],
                     t0, d["xn1_d"], xres)
        sc.wait("sd")
        P.barrier()

    if upto >= 2:
        _ffn1(nc, P, E, d)
    if upto >= 3:
        _qkvz(nc, P, E, d)
    if upto >= 4:
        _conv(nc, P, E, d)
    if upto >= 5:
        _deltanet(nc, P, E, d)
    if upto >= 6:
        d['tail_upto'] = tail_upto
        _tail(nc, P, E, d)

    def reg_setup(ename, eng):
        if ename != "sync":
            return {}
        r0 = eng.register("hs0").__enter__()
        r1 = eng.register("hs1").__enter__()
        eng.reg_load(r0, d["halfsel"][0:1, 0:1])
        eng.reg_load(r1, d["halfsel"][0:1, 1:2])
        return {"idx0": r0, "idx1": r1}

    P.barrier()
    P.emit(nc, reg_setup=reg_setup)
    if upto < 2:
        d["lnstk"].close()
    if upto < 4:
        d["xstk"].close()
    st.close()
    return nc, dram


def _recon_stream(nc, P, E, d, jobs):
    raise NotImplementedError


def _ffn1(nc, P, E, d):
    sy, ve, sc, gp, pe = P.sync, P.vector, P.scalar, P.gpsimd, P.tensor
    ps = d["ps"]; xres = d["xres"]; ident_b = d["ident_b"]
    fw, bw, bw2 = d["fw"], d["bw"], d["bw2"]
    for s in ["ph0", "ph1", "cw", "rc", "rc2", "rc3", "pt", "wcons", "ds", "ac", "ld2a", "ld2b", "wr2"]:
        P.sem(s)

    d["lnstk"].close()
    stA = ExitStack()
    E = stA.enter_context
    phib = [E(nc.sbuf_tensor(f"phib{i}", [128, 4 * 640], F16)) for i in range(2)]
    prodb = [E(nc.sbuf_tensor(f"prodb{i}", [128, 4 * 640], F16)) for i in range(2)]
    cpwg = E(nc.sbuf_tensor("cpwg", [128, FT * 8], F32))
    cpwu = E(nc.sbuf_tensor("cpwu", [128, FT * 8], F32))
    cpwd = E(nc.sbuf_tensor("cpwd", [128, IT * 8], F32))
    wcolg = [E(nc.sbuf_tensor(f"wcolg{i}", [128, FT * 128], BF16)) for i in range(2)]
    wcolu = [E(nc.sbuf_tensor(f"wcolu{i}", [128, FT * 128], BF16)) for i in range(2)]
    wtmp = [E(nc.sbuf_tensor(f"wtmp{i}", [128, 4 * 128], BF16)) for i in range(2)]
    rstage = [E(nc.sbuf_tensor(f"rstage{i}", [128, 4 * 128], BF16)) for i in range(2)]
    sgb = [E(nc.sbuf_tensor(f"sgb{i}", [128, TB], F32)) for i in range(2)]
    abf = [E(nc.sbuf_tensor(f"abf{i}", [128, TB], BF16)) for i in range(4)]

    grp_counter = [0]

    def recon_group(phi_ap, phi_src_tile0, cpw_sb, colbase, dests, dram_dest):
        n = grp_counter[0]
        grp_counter[0] += 1
        par = n % 2
        pb = phib[par]
        pr = prodb[par]
        sy.wait("rc2", max(0, P.cnt["rc2"] - 1))
        phsem = "ph0" if par == 0 else "ph1"
        sy.dma_start(out=pb[:], in_=phi_ap[phi_src_tile0:phi_src_tile0 + 4]
                     .rearrange("t p c -> p t c"))
        sy.inc(phsem, 16)
        ve.wait(phsem)
        ve.wait("rc", max(0, P.cnt["rc"] - 1))
        for t in range(4):
            for j in range(5):
                ve.tensor_scalar(pr[:, t * 640 + j * 128: t * 640 + (j + 1) * 128],
                                 pb[:, t * 640 + j * 128: t * 640 + (j + 1) * 128],
                                 cpw_sb[:, colbase + t * 8 + j: colbase + t * 8 + j + 1],
                                 None, ALU.mult)
        ve.drain()
        v0 = pr[:].rearrange("p (t j c) -> p t j c", t=4, j=5)
        ve.tensor_tensor(v0[:, :, 0, :], v0[:, :, 0, :], v0[:, :, 1, :], ALU.add)
        ve.inc("rc2")
        gp.wait("rc2")
        gp.wait("pt", max(0, P.cnt["pt"] - 4))
        gp.tensor_tensor(v0[:, :, 2, :], v0[:, :, 2, :], v0[:, :, 3, :], ALU.add)
        gp.drain()
        gp.tensor_tensor(v0[:, :, 0, :], v0[:, :, 0, :], v0[:, :, 2, :], ALU.add)
        gp.drain()
        rst = rstage[par]
        gp.tensor_tensor(rst[:].rearrange("p (t c) -> p t c", t=4),
                         v0[:, :, 0, :], v0[:, :, 4, :], ALU.add)
        gp.inc("rc")
        pst = ps[6 + par][:].bitcast(BF16)
        pe.wait("rc", P.cnt["rc"])
        pe.wait("rc3", max(0, P.cnt["rc3"] - 1))
        for t in range(4):
            pe.transpose(pst[:, t * 128:(t + 1) * 128], rst[:, t * 128:(t + 1) * 128], ident_b[:])
            pe.inc("pt")
        sc.wait("pt")
        if dram_dest is not None:
            sc.wait("wr2", P.cnt["wr2"])
        sc.activation(dests, pst[:, 0:512], AF.Copy)
        sc.inc("rc3")
        if dram_dest is not None:
            sc.drain()
            sc.dma_start(out=dram_dest.rearrange("t p c -> p t c"), in_=dests)
            sc.inc("wr2", 16)

    for a in range(IT):
        sy.wait("rc2", P.cnt["rc2"])
        sy.dma_start(out=cpwg[:], in_=d["cpw_g"][a])
        sy.inc("cw", 16)
        sy.dma_start(out=cpwu[:], in_=d["cpw_u"][a])
        sy.inc("cw", 16)
        ve.wait("cw")
        par = a % 2
        if a >= 2:
            gp.wait("ac", (a - 1) * NTB)
            ve.wait("ac", (a - 1) * NTB)
            sc.wait("ac", (a - 1) * NTB)
            sc.wait("wr", P.cnt["wr"])
        sd_before_a = P.cnt["sd"]
        for g4 in range(4):
            recon_group(d["phi_gu"], a * FT + g4 * 4, cpwg, g4 * 32,
                        wcolg[par][:, g4 * 512:(g4 + 1) * 512], None)
            recon_group(d["phi_gu"], a * FT + g4 * 4, cpwu, g4 * 32,
                        wcolu[par][:, g4 * 512:(g4 + 1) * 512], None)
        gp.wait("rc3", P.cnt["rc3"])
        gp.dma_start(out=d["wg_d"][a].rearrange("t p c -> p t c"), in_=wcolg[par][:])
        gp.inc("wr", 16)
        gp.dma_start(out=d["wu_d"][a].rearrange("t p c -> p t c"), in_=wcolu[par][:])
        gp.inc("wr", 16)
        pe.wait("rc3", P.cnt["rc3"])
        for tb in range(NTB):
            idx = a * NTB + tb
            bg, bu = ps[tb % 2], ps[2 + tb % 2]
            if idx >= 2:
                pe.wait("ac", idx - 1)
            for kt in range(FT):
                pe.matmul(bg[:, 0:TB], wcolg[par][:, kt * 128:(kt + 1) * 128],
                          xres[kt][:, tb * TB:(tb + 1) * TB], start=(kt == 0), stop=(kt == FT - 1))
            for kt in range(FT):
                pe.matmul(bu[:, 0:TB], wcolu[par][:, kt * 128:(kt + 1) * 128],
                          xres[kt][:, tb * TB:(tb + 1) * TB], start=(kt == 0), stop=(kt == FT - 1))
            pe.inc("pd")
            sc.wait("pd")
            if idx >= 2:
                sc.wait("ac", idx - 1)
            sc.activation(sgb[tb % 2][:], bg[:, 0:TB], AF.Exp, scale=-1.0)
            sc.drain()
            sc.activation(sgb[tb % 2][:], sgb[tb % 2][:], AF.Ln, bias=d["smalls"][0:128, 5:6])
            sc.drain()
            sc.activation(sgb[tb % 2][:], sgb[tb % 2][:], AF.Exp, scale=-1.0)
            sc.inc("ad")
            ve.wait("ad")
            ve.wait("sd", sd_before_a)
            ve.tensor_tensor(sgb[tb % 2][:], sgb[tb % 2][:], bg[:, 0:TB], ALU.mult)
            ve.drain()
            ve.tensor_tensor(abf[tb][:], sgb[tb % 2][:], bu[:, 0:TB], ALU.mult)
            ve.inc("ac")
            sc.wait("ac")
            sc.dma_start(out=d["a_d"][a, :, tb * TB:(tb + 1) * TB], in_=abf[tb][:])
            sc.inc("sd", 16)
        sc.wait("sd")
    for a in range(FT):
        sy.wait("rc2", P.cnt["rc2"])
        sy.dma_start(out=cpwd[:], in_=d["cpw_d"][a])
        sy.inc("cw", 16)
        ve.wait("cw")
        for g4 in range(IT // 4):
            recon_group(d["phi_dn"], a * IT + g4 * 4, cpwd, g4 * 32,
                        wtmp[(a * 16 + g4) % 2][:], d["wd_d"][a, g4 * 4:(g4 + 1) * 4])
    sc.wait("sd")
    sc.wait("wr2")
    gp.wait("wr")
    P.barrier()
    stA.close()

    stB = ExitStack()
    E = stB.enter_context
    fw = [E(nc.sbuf_tensor(f"fwB{i}", [128, TB], F32)) for i in range(FT)]
    bw = [E(nc.sbuf_tensor(f"bwB{i}", [128, TB], BF16)) for i in range(FT)]
    bw2 = [E(nc.sbuf_tensor(f"bw2B{i}", [128, TB], BF16)) for i in range(FT)]
    rowf = E(nc.sbuf_tensor("rowfB", [1, 4 * TB], F32))
    bcf = E(nc.sbuf_tensor("bcfB", [128, 2 * TB], F32))
    d.update(fw=fw, bw=bw, bw2=bw2, rowf=rowf, bcf=bcf)
    abuf = [E(nc.sbuf_tensor(f"abuf{i}", [128, TB], BF16)) for i in range(2)]
    wdb = [E(nc.sbuf_tensor(f"wdb{i}", [128, 8 * 128], BF16)) for i in range(2)]
    for tb in range(NTB):
        t0 = tb * TB
        if tb > 0:
            sy.wait("vd"); sy.wait("gd"); sy.wait("ad"); sy.wait("pd")
        for i in range(FT):
            sy.dma_start(out=fw[i][:], in_=d["x_fm"][i * 128:(i + 1) * 128, t0:t0 + TB])
            sy.inc("ld", 16)
        for og in range(2):
            g = tb * 2 + og
            pd0 = P.cnt["pd"]
            sy.wait("pd", pd0)
            for kt in range(IT):
                if kt >= 2:
                    sy.wait("pd", pd0 + kt // 2)
                lsem = "ld2a" if kt % 2 == 0 else "ld2b"
                sy.dma_start(out=wdb[kt % 2][:],
                             in_=d["wd_d"][og * 8:(og + 1) * 8, kt].rearrange("o p c -> p o c"))
                sy.inc(lsem, 16)
                sy.dma_start(out=abuf[kt % 2][:], in_=d["a_d"][kt, :, t0:t0 + TB])
                sy.inc(lsem, 16)
                pe.wait(lsem)
                if kt == 0:
                    pe.wait("ld", P.cnt["ld"])
                    if g >= 1:
                        pe.wait("vd2", g)
                    if og == 0 and tb > 0:
                        pe.wait("vd", P.cnt["vd"])
                        pe.wait("gd", P.cnt["gd"])
                for oi in range(8):
                    pe.matmul(ps[oi][:, 0:TB], wdb[kt % 2][:, oi * 128:(oi + 1) * 128],
                              abuf[kt % 2][:], start=(kt == 0), stop=(kt == IT - 1))
                if kt % 2 == 1:
                    pe.inc("pd")
            ve.wait("pd")
            for oi in range(8):
                ot = og * 8 + oi
                ve.scalar_tensor_tensor(fw[ot][:], ps[oi][:, 0:TB], 0.5, fw[ot][:],
                                        ALU.mult, ALU.add)
            ve.inc("vd2")
        ln_src_loaded(nc, P, d, t0, d["xn2_d"], xres, tap_x1=d["x1_d"])
    sc.wait("sd")
    P.barrier()
    stB.close()


def ln_src_loaded(nc, P, d, t0, dst_dram, dst_res, tap_x1=None):
    sy, ve, sc, gp, pe = P.sync, P.vector, P.scalar, P.gpsimd, P.tensor
    ps = d["ps"]; fw, bw, bw2 = d["fw"], d["bw"], d["bw2"]
    rowf, bcf = d["rowf"], d["bcf"]
    onescol_b, onesrow_f = d["onescol_b"], d["onesrow_f"]
    smalls = d["smalls"]
    if tap_x1 is not None:
        sc.wait("vd2", P.cnt["vd2"])
        for i in range(FT):
            sc.dma_start(out=tap_x1[i, :, t0:t0 + TB], in_=fw[i][:])
            sc.inc("sd", 16)
    ve.wait("sd", P.cnt["sd"])
    for i in range(FT):
        ve.tensor_copy(bw[i][:], fw[i][:])
    ve.inc("vd")
    sc.wait("vd", P.cnt["vd"])
    for i in range(FT):
        sc.activation(bw2[i][:], fw[i][:], AF.Square)
    sc.inc("ad")
    pe.wait("vd"); pe.wait("ad")
    for i in range(FT):
        pe.matmul(ps[0][0:1, 0:TB], onescol_b[:], bw[i][:], start=(i == 0), stop=(i == FT - 1))
    for i in range(FT):
        pe.matmul(ps[1][0:1, 0:TB], onescol_b[:], bw2[i][:], start=(i == 0), stop=(i == FT - 1))
    pe.inc("pd")
    ve.wait("pd")
    ve.tensor_scalar(rowf[0:1, 0:TB], ps[0][0:1, 0:TB], 1.0 / HID, None, ALU.mult)
    ve.tensor_scalar(rowf[0:1, TB:2 * TB], ps[1][0:1, 0:TB], 1.0 / HID, None, ALU.mult)
    ve.drain()
    ve.tensor_mul(rowf[0:1, 2 * TB:3 * TB], rowf[0:1, 0:TB], rowf[0:1, 0:TB])
    ve.drain()
    ve.tensor_sub(rowf[0:1, 3 * TB:4 * TB], rowf[0:1, TB:2 * TB], rowf[0:1, 2 * TB:3 * TB])
    ve.inc("vd")
    sc.wait("vd")
    sc.activation(rowf[0:1, 2 * TB:3 * TB], rowf[0:1, 3 * TB:4 * TB], AF.Ln, bias=smalls[0:1, 3:4])
    sc.drain()
    sc.activation(rowf[0:1, TB:2 * TB], rowf[0:1, 2 * TB:3 * TB], AF.Exp, scale=-0.5)
    sc.inc("ad")
    pe.wait("ad")
    pe.matmul(ps[2][:, 0:TB], onesrow_f[:], rowf[0:1, 0:TB], start=True, stop=True)
    pe.matmul(ps[3][:, 0:TB], onesrow_f[:], rowf[0:1, TB:2 * TB], start=True, stop=True)
    pe.inc("pd")
    ve.wait("pd")
    ve.tensor_copy(bcf[:, 0:TB], ps[2][:, 0:TB])
    ve.tensor_copy(bcf[:, TB:2 * TB], ps[3][:, 0:TB])
    ve.inc("vd")
    gp.wait("vd"); gp.wait("sd", P.cnt["sd"])
    for i in range(FT):
        eng = ve if (i % 2 == 0) else gp
        eng.tensor_sub(fw[i][:], fw[i][:], bcf[:, 0:TB])
    ve.fence(); gp.fence()
    for i in range(FT):
        eng = ve if (i % 2 == 0) else gp
        eng.tensor_mul(bw[i][:], fw[i][:], bcf[:, TB:2 * TB])
    if dst_res is not None:
        ve.fence(); gp.fence()
        for i in range(FT):
            eng = ve if (i % 2 == 0) else gp
            eng.tensor_copy(dst_res[i][:, t0:t0 + TB], bw[i][:])
    ve.inc("vd")
    gp.inc("gd")
    if dst_dram is not None:
        sc.wait("vd"); sc.wait("gd")
        for i in range(FT):
            sc.dma_start(out=dst_dram[i, :, t0:t0 + TB], in_=bw[i][:])
            sc.inc("sd", 16)


def _qkvz(nc, P, E, d):
    sy, ve, sc, gp, pe = P.sync, P.vector, P.scalar, P.gpsimd, P.tensor
    ps = d["ps"]; xres = d["xres"]; smalls = d["smalls"]
    for s in ["qw", "qwv", "qwg", "qwa", "qwb", "qe", "qp", "gatep"]:
        P.sem(s)
    Apl, colpl, identf = d["Apl"], d["colpl"], d["identf"]

    stq = ExitStack()
    E2 = stq.enter_context
    beta = E2(nc.sbuf_tensor("beta", [32, T], F32))
    lnb = E2(nc.sbuf_tensor("lnb", [32, T], F32))
    gpl_ = E2(nc.sbuf_tensor("gpl_", [32, T], F32))
    acum = E2(nc.sbuf_tensor("acum", [32, T], F32))
    AeA = E2(nc.sbuf_tensor("AeA", [32, T], F32))
    wqb = [E2(nc.sbuf_tensor(f"wqb{i}", [128, FT * 128], F32)) for i in range(2)]
    wqc = [E2(nc.sbuf_tensor(f"wqc{i}", [128, FT * 128], BF16)) for i in range(2)]
    ob = [E2(nc.sbuf_tensor(f"ob{i}", [128, TB], BF16)) for i in range(4)]
    bpl = E2(nc.sbuf_tensor("bpl", [32, T], F32))
    apl = E2(nc.sbuf_tensor("apl", [32, T], F32))
    scr1 = E2(nc.sbuf_tensor("scr1", [32, T], F32))
    scr2 = E2(nc.sbuf_tensor("scr2", [32, T], F32))

    ve.memset(identf[:], 0.0)
    ve.inc("gatep")
    gp.wait("gatep")
    gp.affine_select(out=identf[:], in_=identf[:], compare_op=ALU.not_equal,
                     fill=1.0, base=0, pattern=[[-1, 128]], channel_multiplier=1)
    gp.inc("gatep")

    def dest_of(oc):
        kh, r = oc // 6, oc % 6
        if r == 0:
            return d["mixed_d"][kh]
        if r == 1:
            return d["mixed_d"][16 + kh]
        if r in (2, 3):
            return d["mixed_d"][32 + 2 * kh + (r - 2)]
        return d["z_d"][2 * kh + (r - 4)]

    for oc in range(96):
        par = oc % 2
        lsem = "qwa" if par == 0 else "qwb"
        sy.wait("qwv", max(0, P.cnt["qwv"] - 1))
        sy.wait("qwg", max(0, P.cnt["qwg"] - 1))
        sy.dma_start(out=wqb[par][:], in_=d["wq_t"][oc].rearrange("t p c -> p t c"))
        sy.inc(lsem, 16)
        ve.wait(lsem); gp.wait(lsem)
        if oc >= 2:
            ve.wait("qp", max(0, 4 * (oc - 1)))
            gp.wait("qp", max(0, 4 * (oc - 1)))
        for t in range(FT):
            eng = ve if t % 2 == 0 else gp
            eng.tensor_copy(wqc[par][:, t * 128:(t + 1) * 128], wqb[par][:, t * 128:(t + 1) * 128])
        ve.inc("qwv"); gp.inc("qwg")
        pe.wait("qwv", P.cnt["qwv"]); pe.wait("qwg", P.cnt["qwg"])
        for tb in range(NTB):
            f = oc * NTB + tb
            bank = ps[f % 4]
            if f >= 4:
                pe.wait("qe", f - 3)
            for kt in range(FT):
                pe.matmul(bank[:, 0:TB], wqc[par][:, kt * 128:(kt + 1) * 128],
                          xres[kt][:, tb * TB:(tb + 1) * TB], start=(kt == 0), stop=(kt == FT - 1))
            pe.inc("qp")
            sc.wait("qp")
            sc.wait("sd", P.cnt["sd"])
            sc.activation(ob[tb][:], bank[:, 0:TB], AF.Copy)
            sc.inc("qe")
            sc.drain()
            sc.dma_start(out=dest_of(oc)[:, tb * TB:(tb + 1) * TB], in_=ob[tb][:])
            sc.inc("sd", 16)

    wbb = E2(nc.sbuf_tensor("wbb", [128, FT * 64], F32))
    wbc = E2(nc.sbuf_tensor("wbc", [128, FT * 64], BF16))
    sy.wait("qwv", P.cnt["qwv"]); sy.wait("qwg", P.cnt["qwg"])
    sy.dma_start(out=wbb[:], in_=d["wba_t"].rearrange("t p c -> p t c"))
    sy.inc("qwa", 16)
    ve.wait("qwa")
    ve.tensor_copy(wbc[:], wbb[:])
    ve.inc("qwv")
    pe.wait("qwv", P.cnt["qwv"])
    pe.wait("qe", P.cnt["qe"])
    for tb in range(NTB):
        for kt in range(FT):
            pe.matmul(ps[4][0:32, 0:TB], wbc[:, kt * 64:kt * 64 + 32],
                      xres[kt][:, tb * TB:(tb + 1) * TB], start=(kt == 0), stop=(kt == FT - 1))
        for kt in range(FT):
            pe.matmul(ps[5][0:32, 0:TB], wbc[:, kt * 64 + 32:kt * 64 + 64],
                      xres[kt][:, tb * TB:(tb + 1) * TB], start=(kt == 0), stop=(kt == FT - 1))
        pe.inc("qp")
        ve.wait("qp")
        ve.tensor_copy(bpl[:, tb * TB:(tb + 1) * TB], ps[4][0:32, 0:TB])
        ve.tensor_copy(apl[:, tb * TB:(tb + 1) * TB], ps[5][0:32, 0:TB])
        ve.inc("qe")
        pe.wait("qe")

    ve.drain()
    sc.wait("qe", P.cnt["qe"])
    sc.activation(scr1[:], bpl[:], AF.Abs)
    sc.drain()
    sc.activation(scr1[:], scr1[:], AF.Exp, scale=-1.0)
    sc.drain()
    sc.activation(scr1[:], scr1[:], AF.Ln, bias=smalls[0:32, 5:6])
    sc.inc("gatep")
    ve.wait("gatep")
    ve.tensor_scalar(scr2[:], bpl[:], -1.0, 0.0, ALU.mult, ALU.max)
    ve.drain()
    ve.tensor_add(scr2[:], scr2[:], scr1[:])
    ve.drain()
    ve.tensor_scalar(lnb[:], scr2[:], -1.0, None, ALU.mult)
    ve.drain()
    ve.tensor_scalar(scr2[:], apl[:], smalls[0:32, 0:1], None, ALU.add)
    ve.inc("gatep")
    sc.wait("gatep")
    sc.activation(beta[:], lnb[:], AF.Exp)
    sc.activation(scr1[0:32, 0:1], smalls[0:32, 1:2], AF.Exp)
    sc.drain()
    sc.activation(apl[:], scr2[:], AF.Abs)
    sc.drain()
    sc.activation(apl[:], apl[:], AF.Exp, scale=-1.0)
    sc.drain()
    sc.activation(apl[:], apl[:], AF.Ln, bias=smalls[0:32, 5:6])
    sc.inc("gatep")
    ve.wait("gatep")
    ve.tensor_scalar(scr2[:], scr2[:], 0.0, None, ALU.max)
    ve.drain()
    ve.tensor_add(scr2[:], scr2[:], apl[:])
    ve.drain()
    ve.tensor_scalar(gpl_[:], scr2[:], scr1[0:32, 0:1], None, ALU.mult)
    ve.drain()
    ve.tensor_scalar(gpl_[:], gpl_[:], -1.0, None, ALU.mult)
    ve.drain()
    for c in range(NCH):
        slc = slice(c * C, (c + 1) * C)
        ve.tensor_tensor_scan(acum[:, slc], gpl_[:, slc], gpl_[:, slc], 0.0,
                              ALU.add, ALU.bypass)
    ve.drain()
    ve.tensor_add(scr2[:], acum[:], lnb[:])
    ve.inc("gatep")
    sc.wait("gatep")
    sc.activation(Apl[:], acum[:], AF.Exp)
    sc.drain()
    for c in range(NCH):
        slc = slice(c * C, (c + 1) * C)
        sc.activation(AeA[:, slc], acum[:, slc], AF.Exp, scale=-1.0,
                      bias=acum[0:32, c * C + C - 1:c * C + C])
    sc.inc("gatep")
    sc.drain()
    sc.dma_start(out=d["rows_d"][0], in_=acum[:])
    sc.inc("sd", 16)
    sc.dma_start(out=d["rows_d"][1], in_=scr2[:])
    sc.inc("sd", 16)
    sc.dma_start(out=d["Arow_d"], in_=Apl[:])
    sc.inc("sd", 16)
    sc.dma_start(out=d["gates_d"][0:32, :], in_=beta[:])
    sc.inc("sd", 16)
    sc.dma_start(out=d["gates_d"][32:64, :], in_=gpl_[:])
    sc.inc("sd", 16)
    sc.dma_start(out=d["gates_d"][64:96, :], in_=acum[:])
    sc.inc("sd", 16)
    sc.dma_start(out=d["planes_d"][0:32, :], in_=lnb[:])
    sc.inc("sd", 16)

    tmpc = E2(nc.sbuf_tensor("tmpc", [128, 32], F32))
    pe.wait("gatep", P.cnt["gatep"])
    for c in range(NCH):
        slc = slice(c * C, (c + 1) * C)
        base = c * 192
        bank = ps[c % 2]
        if c >= 2:
            pe.wait("qe", P.cnt["qe"])
        pe.transpose(bank[:, 0:32], acum[0:32, slc], identf[0:32, 0:32])
        pe.transpose(bank[:, 32:64], AeA[0:32, slc], identf[0:32, 0:32])
        pe.transpose(bank[:, 64:96], lnb[0:32, slc], identf[0:32, 0:32])
        pe.transpose(bank[:, 96:128], beta[0:32, slc], identf[0:32, 0:32])
        pe.inc("qp")
        sc.wait("qp")
        sc.wait("qe", P.cnt["qe"])
        sc.activation(tmpc[:], bank[:, 0:32], AF.Exp)
        sc.inc("ad")
        ve.wait("qp")
        ve.tensor_copy(colpl[:, base:base + 32], bank[:, 0:32])
        ve.tensor_scalar(colpl[:, base + 32:base + 64], bank[:, 0:32], -1.0, None, ALU.mult)
        ve.drain()
        ve.tensor_add(colpl[:, base + 64:base + 96], colpl[:, base:base + 32], bank[:, 64:96])
        ve.tensor_copy(colpl[:, base + 96:base + 128], bank[:, 96:128])
        ve.tensor_copy(colpl[:, base + 160:base + 192], bank[:, 32:64])
        ve.wait("ad")
        ve.drain()
        ve.tensor_mul(colpl[:, base + 128:base + 160], colpl[:, base + 96:base + 128], tmpc[:])
        ve.inc("qe")
    sc.wait("sd")
    P.barrier()
    stq.close()

def _conv(nc, P, E, d):
    sy, ve, sc, gp, pe = P.sync, P.vector, P.scalar, P.gpsimd, P.tensor
    ps = d["ps"]; smalls = d["smalls"]; convw = d["convw"]
    for s in ["cv0", "cv1", "cvt", "cve", "cvq", "cvs", "cvp", "cst", "cvg", "cvn"]:
        P.sem(s)
    d["xstk"].close()
    stc = ExitStack()
    E2 = stc.enter_context
    mb = [E2(nc.sbuf_tensor(f"mb{i}", [128, T + 4], BF16)) for i in range(2)]
    accA = E2(nc.sbuf_tensor("accA", [128, T], F32))
    accB = E2(nc.sbuf_tensor("accB", [128, T], F32))
    sqb = E2(nc.sbuf_tensor("sqb", [128, T], BF16))
    outb = [E2(nc.sbuf_tensor(f"outb{i}", [128, T], BF16)) for i in range(2)]
    rowb = E2(nc.sbuf_tensor("rowb", [1, 2 * TB], F32))
    rsb = E2(nc.sbuf_tensor("rsb", [1, TB], BF16))

    for ct in range(64):
        par = ct % 2
        lsem = "cv0" if par == 0 else "cv1"
        sy.wait("cvt", max(0, P.cnt["cvt"] - 1))
        sy.dma_start(out=mb[par][:, 3:T + 3], in_=d["mixed_d"][ct])
        sy.inc(lsem, 16)
        if ct < 2:
            ve.memset(mb[par][:, 0:3], 0.0)
            ve.drain()
        ve.wait(lsem)
        ve.wait("cve", max(0, P.cnt["cve"] - 1))
        ve.wait("cst", P.cnt["cst"])
        ve.drain()
        wc = convw[:, ct * 4:(ct + 1) * 4]
        ve.tensor_scalar(accA[:], mb[par][:, 0:T], wc[:, 0:1], None, ALU.mult)
        ve.tensor_scalar(accB[:], mb[par][:, 1:1 + T], wc[:, 1:2], None, ALU.mult)
        ve.drain()
        ve.scalar_tensor_tensor(accA[:], mb[par][:, 2:2 + T], wc[:, 2:3], accA[:],
                                ALU.mult, ALU.add)
        ve.scalar_tensor_tensor(accB[:], mb[par][:, 3:3 + T], wc[:, 3:4], accB[:],
                                ALU.mult, ALU.add)
        ve.drain()
        ve.tensor_add(accA[:], accA[:], accB[:])
        ve.inc("cvt")
        sc.wait("cvt")
        if ct >= 32:
            sc.wait("cst", P.cnt["cst"])
            sc.activation(accB[:], accA[:], AF.Exp, scale=-1.0)
            sc.drain()
            sc.activation(accB[:], accB[:], AF.Ln, bias=smalls[0:128, 5:6])
            sc.drain()
            sc.activation(accB[:], accB[:], AF.Exp, scale=-1.0)
            sc.inc("cve")
            ve.wait("cve")
            ve.tensor_tensor(outb[par][:], accA[:], accB[:], ALU.mult)
            ve.inc("cvn")
            sc.wait("cvn")
            sc.dma_start(out=d["vc_d"][ct - 32], in_=outb[par][:])
            sc.inc("cst", 16)
        else:
            sc.activation(accB[:], accA[:], AF.Exp, scale=-1.0)
            sc.drain()
            sc.activation(accB[:], accB[:], AF.Ln, bias=smalls[0:128, 5:6])
            sc.drain()
            sc.activation(accB[:], accB[:], AF.Exp, scale=-1.0)
            sc.inc("cvg")
            ve.wait("cvg")
            ve.tensor_tensor(accB[:], accA[:], accB[:], ALU.mult)
            ve.inc("cvn")
            sc.wait("cvn")
            sc.activation(sqb[:], accB[:], AF.Square)
            sc.inc("cve")
            pe.wait("cve", P.cnt["cve"])
            for q4 in range(NTB):
                if P.cnt["cvq"] > 0:
                    pe.wait("cvq", P.cnt["cvq"])
                pe.matmul(ps[4][0:1, 0:TB], d["onescol_b"][:], sqb[:, q4 * TB:(q4 + 1) * TB],
                          start=True, stop=True)
                pe.inc("cvp")
                ve.wait("cvp")
                ve.tensor_scalar(rowb[0:1, 0:TB], ps[4][0:1, 0:TB], 1.0 / DK, None, ALU.mult)
                ve.inc("cvq")
                sc.wait("cvq")
                sc.activation(rowb[0:1, TB:2 * TB], rowb[0:1, 0:TB], AF.Ln,
                              bias=smalls[0:1, 4:5])
                sc.drain()
                sc.activation(rsb[0:1, 0:TB], rowb[0:1, TB:2 * TB], AF.Exp, scale=-0.5,
                              bias=smalls[0:1, 6:7] if ct < 16 else smalls[0:1, 7:8])
                sc.inc("cvs")
                pe.wait("cvs")
                pe.matmul(ps[5][:, 0:TB], d["onesrow_b"][:], rsb[0:1, 0:TB],
                          start=True, stop=True)
                pe.inc("cvp")
                ve.wait("cvp")
                ve.tensor_tensor(outb[par][:, q4 * TB:(q4 + 1) * TB],
                                 accB[:, q4 * TB:(q4 + 1) * TB], ps[5][:, 0:TB], ALU.mult)
                ve.inc("cvq")
            sc.wait("cvq", P.cnt["cvq"])
            sc.wait("cst", P.cnt["cst"])
            sc.drain()
            dst = d["qn_d"][ct] if ct < 16 else d["kn_d"][ct - 16]
            sc.dma_start(out=dst, in_=outb[par][:])
            sc.inc("cst", 16)
    sc.wait("cst")
    P.barrier()
    stc.close()

def _deltanet(nc, P, E, d):
    sy, ve, sc, gp, pe = P.sync, P.vector, P.scalar, P.gpsimd, P.tensor
    ps = d["ps"]; smalls = d["smalls"]; colpl = d["colpl"]
    maskLn, maskUn, maskUi = d["maskLn"], d["maskUn"], d["maskUi"]
    ident_b = d["ident_b"]
    onescol_b, onesrow_b = d["onescol_b"], d["onesrow_b"]
    sems = ["dla", "dlb", "dlh0", "dlh1", "dp1", "dv1", "da1", "dv2", "dpc",
            "dvc", "dp3", "dv3", "dp4", "dv4", "dp5", "dv5", "da2", "dp6",
            "dv6", "da3", "dp7", "dv7", "dst", "dgp"]
    for s in sems:
        P.sem(s)
    std = ExitStack()
    E2 = std.enter_context
    S = [E2(nc.sbuf_tensor(f"S{h}", [128, 128], F32)) for h in range(NV)]
    def slot_bufs(sl):
        mk = lambda n, c, dt_: E2(nc.sbuf_tensor(f"{n}{sl}", [128, c], dt_))
        return dict(
            kt=mk("d_kt", 128, BF16), qt=mk("d_qt", 128, BF16), vt=mk("d_vt", 128, BF16),
            abc=mk("d_abc", 128, F32), sbc=mk("d_sbc", 128, F32),
            Abc=mk("d_Abc", 128, BF16), zb=mk("d_zb", 128, BF16),
            vtm=mk("d_vtm", 128, BF16), ktm=mk("d_ktm", 128, BF16), ktl=mk("d_ktl", 128, BF16),
            eb=mk("d_eb", 128, F32), ebT=mk("d_ebT", 128, F32), egT=mk("d_egT", 128, F32),
            cl1=mk("d_cl1", 128, F32), cl2=mk("d_cl2", 128, F32), cl3=mk("d_cl3", 128, F32),
            nb=mk("d_nb", 128, BF16), ntb=mk("d_ntb", 128, BF16), pn=mk("d_pn", 256, BF16),
            tt=mk("d_tt", 128, BF16), r0=mk("d_r0", 128, BF16), ub=mk("d_ub", 128, BF16),
            gt=mk("d_gt", 128, BF16), qa=mk("d_qa", 128, BF16), sb=mk("d_sb", 128, BF16),
            ob=mk("d_ob", 128, BF16), sq=mk("d_sq", 128, BF16), sz=mk("d_sz", 128, BF16),
            go=mk("d_go", 128, BF16), aec=mk("d_aec", 1, F32),
            rr=E2(nc.sbuf_tensor(f"d_rr{sl}", [1, 256], F32)),
            rsb=E2(nc.sbuf_tensor(f"d_rsb{sl}", [1, 128], BF16)),
        )
    SB = [slot_bufs(0), slot_bufs(1)]

    for h in range(NV):
        (ve if h % 2 == 0 else gp).memset(S[h][:], 0.0)
    ve.inc("dv5"); gp.inc("dgp")
    ve.drain(); gp.drain()

    HG = 4
    NGRP = NV // HG
    inst = 0
    for hg in range(NGRP):
        for c in range(NCH):
            for i in range(HG):
                hv = hg * HG + i
                kh = hv // 2
                sl = inst % 2
                B = SB[sl]
                slc = slice(c * C, (c + 1) * C)
                cb = c * 192
                a_col = colpl[:, cb + hv:cb + hv + 1]
                na_col = colpl[:, cb + 32 + hv:cb + 32 + hv + 1]
                s_col = colpl[:, cb + 64 + hv:cb + 64 + hv + 1]
                b_col = colpl[:, cb + 96 + hv:cb + 96 + hv + 1]
                bA_col = colpl[:, cb + 128 + hv:cb + 128 + hv + 1]
                AeA_col = colpl[:, cb + 160 + hv:cb + 160 + hv + 1]
                j = inst
                ls = "dla" if sl == 0 else "dlb"
                if j >= 2:
                    sy.wait("dp3", j - 1)
                    sy.wait("dv1", j - 1)
                sy.dma_start(out=B["kt"][:], in_=d["kn_d"][kh, :, slc])
                sy.inc(ls, 16)
                sy.dma_start(out=B["qt"][:], in_=d["qn_d"][kh, :, slc])
                sy.inc(ls, 16)
                sy.dma_start(out=B["vt"][:], in_=d["vc_d"][hv, :, slc])
                sy.inc(ls, 16)
                if j >= 2:
                    sy.wait("da1", j - 1)
                    sy.wait("dv7", j - 1)
                sy.dma_start(out=B["abc"][:], in_=d["rows_d"][0, hv:hv + 1, slc].broadcast_to([128, C]))
                sy.inc(ls, 16)
                sy.dma_start(out=B["sbc"][:], in_=d["rows_d"][1, hv:hv + 1, slc].broadcast_to([128, C]))
                sy.inc(ls, 16)
                sy.dma_start(out=B["Abc"][:], in_=d["Arow_d"][hv:hv + 1, slc].broadcast_to([128, C]))
                sy.inc(ls, 16)
                sy.dma_start(out=B["zb"][:], in_=d["z_d"][hv, :, slc])
                sy.inc(ls, 16)
                bA_ = ps[4 * sl + 0]
                bB_ = ps[4 * sl + 1]
                bC_ = ps[4 * sl + 2]
                bD_ = ps[4 * sl + 3]
                pe.wait(ls, P.cnt[ls])
                if j >= 2:
                    pe.wait("dv2", j - 1)
                    pe.wait("dv3", j - 1)
                pe.matmul(bA_[:, 0:128], B["kt"][:], B["kt"][:], start=True, stop=True)
                pe.matmul(bA_[:, 128:256], B["kt"][:], B["qt"][:], start=True, stop=True)
                pCb = bC_[:].bitcast(BF16)
                pe.transpose(pCb[:, 0:128], B["vt"][:], ident_b[:])
                pe.transpose(pCb[:, 128:256], B["kt"][:], ident_b[:])
                pe.inc("dp1")
                ve.wait("dp1")
                if j >= 2:
                    ve.wait("da1", j - 1)
                    ve.wait("dv7", j - 1)
                ve.tensor_scalar(B["cl1"][:], B["abc"][:], s_col, 0.0, ALU.subtract, ALU.max)
                ve.tensor_scalar(B["cl2"][:], B["sbc"][:], a_col, 0.0, ALU.subtract, ALU.min)
                ve.tensor_scalar(B["cl3"][:], B["abc"][:], a_col, 0.0, ALU.subtract, ALU.min)
                ve.tensor_tensor(B["qa"][:], B["qt"][:], B["Abc"][:], ALU.mult)
                if j >= HG:
                    ve.wait("dv5", j - HG + 2)
                else:
                    ve.wait("dgp", 1)
                ve.tensor_copy(B["sb"][:], S[hv][:])
                ve.inc("dv1")
                sc.wait("dv1")
                sc.wait("dp1", P.cnt["dp1"])
                sc.activation(B["eb"][:], B["cl1"][:], AF.Exp, scale=-1.0)
                sc.activation(B["ebT"][:], B["cl2"][:], AF.Exp)
                sc.activation(B["egT"][:], B["cl3"][:], AF.Exp)
                sc.activation(B["vtm"][:], pCb[:, 0:128], AF.Copy)
                sc.activation(B["ktm"][:], pCb[:, 128:256], AF.Copy)
                sc.activation(B["aec"][:], B["abc"][:, C - 1:C], AF.Exp)
                sc.inc("da1")
                ve.wait("da1")
                ve.tensor_tensor(B["cl1"][:], bA_[:, 0:128], B["eb"][:], ALU.mult)
                ve.tensor_tensor(B["cl2"][:], bA_[:, 0:128], B["ebT"][:], ALU.mult)
                ve.tensor_tensor(B["cl3"][:], bA_[:, 128:256], B["egT"][:], ALU.mult)
                ve.drain()
                ve.tensor_tensor(B["nb"][:], B["cl1"][:], maskLn[:], ALU.mult)
                ve.tensor_tensor(B["ntb"][:], B["cl2"][:], maskUn[:], ALU.mult)
                ve.tensor_tensor(B["gt"][:], B["cl3"][:], maskUi[:], ALU.mult)
                ve.tensor_scalar(B["ktl"][:], B["ktm"][:], AeA_col, None, ALU.mult)
                ve.tensor_copy(B["pn"][:, 0:128], ident_b[:])
                ve.drain()
                ve.tensor_copy(B["pn"][:, 128:256], B["nb"][:])
                ve.inc("dv2")
                pe.wait("dv2", P.cnt["dv2"])
                if j >= 2:
                    pe.wait("dv7", j - 1)
                for st in range(7):
                    if st < 6:
                        pe.matmul(bB_[:, 0:256], B["ntb"][:], B["pn"][:], start=True, stop=True)
                        pe.matmul(bB_[:, 256:384], B["nb"][:], B["ntb"][:], start=True, stop=True)
                        pe.inc("dpc")
                        ve.wait("dpc")
                        ve.tensor_add(B["pn"][:, 0:128], B["pn"][:, 0:128], bB_[:, 0:128])
                        ve.tensor_copy(B["nb"][:], bB_[:, 128:256])
                        ve.tensor_copy(B["ntb"][:], bB_[:, 256:384])
                        ve.drain()
                        ve.tensor_copy(B["pn"][:, 128:256], B["nb"][:])
                        ve.inc("dvc")
                        pe.wait("dvc")
                    else:
                        pe.matmul(bB_[:, 0:128], B["ntb"][:], B["pn"][:, 0:128],
                                  start=True, stop=True)
                        pe.inc("dpc")
                        ve.wait("dpc")
                        ve.tensor_add(B["pn"][:, 0:128], B["pn"][:, 0:128], bB_[:, 0:128])
                        ve.inc("dvc")
                        pe.wait("dvc")
                if j >= 2:
                    pe.wait("dv5", j)
                    pe.wait("da2", j - 1)
                pe.transpose(pCb[:, 256:384], B["pn"][:, 0:128], ident_b[:])
                pe.matmul(bD_[:, 0:128], B["kt"][:], B["sb"][:], start=True, stop=True)
                pe.inc("dp3")
                ve.wait("dp3")
                ve.tensor_copy(B["tt"][:], pCb[:, 256:384])
                ve.tensor_scalar(B["cl1"][:], bD_[:, 0:128], bA_col, None, ALU.mult)
                ve.drain()
                ve.scalar_tensor_tensor(B["r0"][:], B["vtm"][:], b_col, B["cl1"][:],
                                        ALU.mult, ALU.subtract)
                ve.inc("dv3")
                pe.wait("dv3")
                pe.matmul(bD_[:, 128:256], B["tt"][:], B["r0"][:], start=True, stop=True)
                pe.inc("dp4")
                ve.wait("dp4")
                ve.tensor_copy(B["ub"][:], bD_[:, 128:256])
                ve.inc("dv4")
                pe.wait("dv4")
                pe.matmul(bD_[:, 256:384], B["ktl"][:], B["ub"][:], start=True, stop=True)
                pe.matmul(bD_[:, 384:512], B["sb"][:], B["qa"][:], start=True, stop=False)
                pe.matmul(bD_[:, 384:512], B["ub"][:], B["gt"][:], start=False, stop=True)
                pe.inc("dp5")
                ve.wait("dp5")
                ve.tensor_copy(B["ob"][:], bD_[:, 384:512])
                ve.scalar_tensor_tensor(S[hv][:], S[hv][:], B["aec"][0:128, 0:1],
                                        bD_[:, 256:384], ALU.mult, ALU.add)
                ve.inc("dv5")
                sc.wait("dv5", j + 2)
                sc.activation(B["sq"][:], B["ob"][:], AF.Square)
                sc.activation(B["cl3"][:], B["zb"][:], AF.Exp, scale=-1.0)
                sc.drain()
                sc.activation(B["cl3"][:], B["cl3"][:], AF.Ln, bias=smalls[0:128, 5:6])
                sc.drain()
                sc.activation(B["sz"][:], B["cl3"][:], AF.Exp, scale=-1.0)
                sc.inc("da2")
                pe.wait("da2")
                if j >= 2:
                    pe.wait("dv6", j - 1)
                pe.matmul(bA_[0:1, 256:384], onescol_b[:], B["sq"][:], start=True, stop=True)
                pe.inc("dp6")
                ve.wait("dp6")
                ve.tensor_scalar(B["rr"][0:1, 0:128], bA_[0:1, 256:384], 1.0 / DV, None, ALU.mult)
                ve.inc("dv6")
                sc.wait("dv6")
                sc.activation(B["rr"][0:1, 128:256], B["rr"][0:1, 0:128], AF.Ln,
                              bias=smalls[0:1, 4:5])
                sc.drain()
                sc.activation(B["rsb"][0:1, 0:128], B["rr"][0:1, 128:256], AF.Exp, scale=-0.5)
                sc.inc("da3")
                pe.wait("da3")
                pe.matmul(bB_[:, 0:128], onesrow_b[:], B["rsb"][0:1, 0:128],
                          start=True, stop=True)
                pe.inc("dp7")
                ve.wait("dp7")
                ve.tensor_tensor(B["cl2"][:], B["ob"][:], bB_[:, 0:128], ALU.mult)
                ve.drain()
                ve.tensor_scalar(B["cl2"][:], B["cl2"][:], smalls[:, 2:3], None, ALU.mult)
                ve.drain()
                ve.tensor_tensor(B["go"][:], B["cl2"][:], B["sz"][:], ALU.mult)
                ve.drain()
                ve.tensor_tensor(B["go"][:], B["go"][:], B["zb"][:], ALU.mult)
                ve.inc("dv7")
                sc.wait("dv7")
                sc.wait("dst", P.cnt["dst"])
                sc.drain()
                sc.dma_start(out=d["gated_d"][hv, :, slc], in_=B["go"][:])
                sc.inc("dst", 16)
                inst += 1
    sc.wait("dst")
    P.barrier()
    std.close()

def _tail(nc, P, E, d):
    sy, ve, sc, gp, pe = P.sync, P.vector, P.scalar, P.gpsimd, P.tensor
    ps = d["ps"]; smalls = d["smalls"]
    for s in ["tl", "tw", "tg", "te", "tp", "tv", "yd", "t2a", "t2b"]:
        P.sem(s)
    stT = ExitStack()
    E2 = stT.enter_context
    fwbig = E2(nc.sbuf_tensor("fwTbig", [128, FT * TB], F32))
    fw = [fwbig[:, i * TB:(i + 1) * TB] for i in range(FT)]
    bw = [E2(nc.sbuf_tensor(f"bwT{i}", [128, TB], BF16)) for i in range(FT)]
    bw2 = [E2(nc.sbuf_tensor(f"bw2T{i}", [128, TB], BF16)) for i in range(FT)]
    rowf = E2(nc.sbuf_tensor("rowfT", [1, 4 * TB], F32))
    bcf = E2(nc.sbuf_tensor("bcfT", [128, 2 * TB], F32))
    d.update(fw=fw, bw=bw, bw2=bw2, rowf=rowf, bcf=bcf)
    xres3 = [E2(nc.sbuf_tensor(f"xres3_{i}", [128, TH], BF16)) for i in range(FT)]
    gbufbig = E2(nc.sbuf_tensor("gbufbig", [128, 32 * TB], BF16))
    gbuf = [gbufbig[:, i * TB:(i + 1) * TB] for i in range(32)]
    wof = E2(nc.sbuf_tensor("wof", [128, 16 * 128], F32))
    wob = E2(nc.sbuf_tensor("wob", [128, 32 * 128], BF16))

    x1g = d["x1_d"]
    gg = d["gated_d"]

    for tb2 in range(2):
        t0 = tb2 * TB
        ridx = f"idx{tb2}"
        if tb2 > 0:
            sy.wait("vd"); sy.wait("gd"); sy.wait("ad"); sy.wait("tp")
        import bass_rust as _br
        def dynts(r, name):
            return bass.ts(_br.RuntimeValue(val=r[name], min_val=0, max_val=3), TB)
        sy.dma_start(out=fwbig[:],
                     in_=lazy(lambda r, ridx=ridx: x1g[:, :, dynts(r, ridx)]
                              .rearrange("f p t -> p f t")))
        sy.inc("tl", 16)
        sy.dma_start(out=gbufbig[:],
                     in_=lazy(lambda r, ridx=ridx: gg[:, :, dynts(r, ridx)]
                              .rearrange("h p t -> p h t")))
        sy.inc("tl", 16)
        for ot in range(FT):
            for h2 in range(2):
                sy.wait("tg", P.cnt["tg"]); sy.wait("dgp", P.cnt["dgp"])
                sy.dma_start(out=wof[:], in_=d["wo_t"][ot, h2 * 16:(h2 + 1) * 16]
                             .rearrange("t p c -> p t c"))
                sy.inc("tw", 16)
                ve.wait("tw"); gp.wait("tw")
                ve.wait("tp", P.cnt["tp"])
                gp.wait("tp", P.cnt["tp"])
                for t in range(16):
                    eng = ve if t % 2 == 0 else gp
                    eng.tensor_copy(wob[:, (h2 * 16 + t) * 128:(h2 * 16 + t + 1) * 128],
                                    wof[:, t * 128:(t + 1) * 128])
                ve.inc("tg"); gp.inc("dgp")
            pe.wait("tg", P.cnt["tg"]); pe.wait("dgp", P.cnt["dgp"])
            if ot == 0:
                pe.wait("tl", P.cnt["tl"])
                if tb2 == 1:
                    pe.wait("vd", P.cnt["vd"])
            bank = ps[ot % 2]
            if ot >= 2:
                pe.wait("tv", P.cnt["tv"])
            for kt in range(32):
                pe.matmul(bank[:, 0:TB], wob[:, kt * 128:(kt + 1) * 128], gbuf[kt][:],
                          start=(kt == 0), stop=(kt == 31))
            pe.inc("tp")
            ve.wait("tp")
            ve.tensor_add(fw[ot][:], fw[ot][:], bank[:, 0:TB])
            ve.inc("tv")
            sc.wait("tv", P.cnt["tv"])
            sc.dma_start(out=d["x2_d"][ot, :, t0:t0 + TB], in_=fw[ot][:])
            sc.inc("sd", 16)
        ln_src_loaded(nc, P, d, t0, None, xres3)
    if d.get("tail_upto", 99) < 2:
        P.barrier()
        stT.close()
        return
    wcg = [E2(nc.sbuf_tensor(f"wcgT{i}", [128, FT * 128], BF16)) for i in range(2)]
    wcu = [E2(nc.sbuf_tensor(f"wcuT{i}", [128, FT * 128], BF16)) for i in range(2)]
    sgb = E2(nc.sbuf_tensor("sgbT", [128, TB], F32))
    abf = [E2(nc.sbuf_tensor(f"abfT{i}", [128, TB], BF16)) for i in range(2)]
    sy.wait("vd"); sy.wait("gd"); sy.wait("ad"); sy.wait("tp")
    for a in range(IT):
        par = a % 2
        lsem = "t2a" if par == 0 else "t2b"
        if a >= 2:
            sy.wait("tp", P.cnt["tp"])
        sy.dma_start(out=wcg[par][:], in_=d["wg_d"][a].rearrange("t p c -> p t c"))
        sy.inc(lsem, 16)
        sy.dma_start(out=wcu[par][:], in_=d["wu_d"][a].rearrange("t p c -> p t c"))
        sy.inc(lsem, 16)
        pe.wait(lsem, P.cnt[lsem])
        for tb2 in range(2):
            f = a * 2 + tb2
            bg, bu = ps[2 + tb2], ps[4 + tb2]
            if f >= 2:
                pe.wait("te", f - 1)
            for kt in range(FT):
                pe.matmul(bg[:, 0:TB], wcg[par][:, kt * 128:(kt + 1) * 128],
                          xres3[kt][:, tb2 * TB:(tb2 + 1) * TB], start=(kt == 0), stop=(kt == FT - 1))
            for kt in range(FT):
                pe.matmul(bu[:, 0:TB], wcu[par][:, kt * 128:(kt + 1) * 128],
                          xres3[kt][:, tb2 * TB:(tb2 + 1) * TB], start=(kt == 0), stop=(kt == FT - 1))
            pe.inc("tp")
            sc.wait("tp")
            if f >= 2:
                sc.wait("te", f - 1)
            sc.activation(sgb[:], bg[:, 0:TB], AF.Exp, scale=-1.0)
            sc.drain()
            sc.activation(sgb[:], sgb[:], AF.Ln, bias=smalls[0:128, 5:6])
            sc.drain()
            sc.activation(sgb[:], sgb[:], AF.Exp, scale=-1.0)
            sc.inc("ad")
            ve.wait("ad")
            ve.wait("sd", P.cnt["sd"])
            ve.tensor_tensor(sgb[:], sgb[:], bg[:, 0:TB], ALU.mult)
            ve.drain()
            ve.tensor_tensor(abf[tb2 % 2][:], sgb[:], bu[:, 0:TB], ALU.mult)
            ve.inc("te")
            sc.wait("te")
            sc.dma_start(out=d["a_d"][a, :, t0_for(tb2)], in_=abf[tb2 % 2][:])
            sc.inc("sd", 16)
        sc.wait("sd")
    sc.wait("sd")
    P.barrier()
    if d.get("tail_upto", 99) < 3:
        stT.close()
        return
    wdb = [E2(nc.sbuf_tensor(f"wdbT{i}", [128, 8 * 128], BF16)) for i in range(2)]
    abuf = [E2(nc.sbuf_tensor(f"abufT{i}", [128, TB], BF16)) for i in range(2)]
    for tb2 in range(2):
        t0 = tb2 * TB
        if tb2 > 0:
            sy.wait("tv", P.cnt["tv"]); sy.wait("pd", P.cnt["pd"])
        for i in range(FT):
            sy.dma_start(out=fw[i][:], in_=d["x2_d"][i, :, t0:t0 + TB])
            sy.inc("ld", 16)
        for og in range(2):
            g = tb2 * 2 + og
            pd0 = P.cnt["pd"]
            sy.wait("pd", pd0)
            for kt in range(IT):
                if kt >= 2:
                    sy.wait("pd", pd0 + kt // 2)
                lsem = "ld2a" if kt % 2 == 0 else "ld2b"
                sy.dma_start(out=wdb[kt % 2][:],
                             in_=d["wd_d"][og * 8:(og + 1) * 8, kt].rearrange("o p c -> p o c"))
                sy.inc(lsem, 16)
                sy.dma_start(out=abuf[kt % 2][:], in_=d["a_d"][kt, :, t0:t0 + TB])
                sy.inc(lsem, 16)
                pe.wait(lsem)
                if kt == 0:
                    pe.wait("ld", P.cnt["ld"])
                    if g >= 1:
                        pe.wait("tv", P.cnt["tv"])
                for oi in range(8):
                    pe.matmul(ps[oi][:, 0:TB], wdb[kt % 2][:, oi * 128:(oi + 1) * 128],
                              abuf[kt % 2][:], start=(kt == 0), stop=(kt == IT - 1))
                if kt % 2 == 1:
                    pe.inc("pd")
            ve.wait("pd")
            for oi in range(8):
                ot = og * 8 + oi
                ve.scalar_tensor_tensor(fw[ot][:], ps[oi][:, 0:TB], 0.5, fw[ot][:],
                                        ALU.mult, ALU.add)
            ve.inc("tv")
            sc.wait("tv", P.cnt["tv"])
            for oi in range(8):
                ot = og * 8 + oi
                sc.dma_start(out=d["y_fm"][ot * 128:(ot + 1) * 128, t0:t0 + TB], in_=fw[ot][:])
                sc.inc("yd", 16)
    sc.wait("yd")
    sy.wait("yd", P.cnt["yd"])
    P.barrier()
    stT.close()


def t0_for(tb2):
    return slice(tb2 * TB, (tb2 + 1) * TB)


def _host_inputs(L0_attn_x, L0_attn_W_qkvz, L0_attn_W_ba, L0_attn_conv_w,
                 L0_attn_dt_bias, L0_attn_A_log, L0_attn_norm_w, L0_attn_W_out,
                 L0_ffn_gate_cp, L0_ffn_up_cp, L0_ffn_down_cp):
    x = np.asarray(L0_attn_x, np.float32)
    phi_g, _ = _phi_w0(HID)
    phi_d, _ = _phi_w0(INTER)
    cpg = _cpwin(np.asarray(L0_ffn_gate_cp, np.float32), HID)
    cpu_ = _cpwin(np.asarray(L0_ffn_up_cp, np.float32), HID)
    cpd = _cpwin(np.asarray(L0_ffn_down_cp, np.float32), INTER)
    wq_t = _tile_w(np.asarray(L0_attn_W_qkvz, np.float32))
    wo_t = _tile_w(np.asarray(L0_attn_W_out, np.float32))
    wba = np.asarray(L0_attn_W_ba, np.float32).reshape(HID, NK, 2, 2)
    wba_p = np.zeros((HID, 64), np.float32)
    for kh in range(NK):
        for r in range(2):
            hv = 2 * kh + r
            wba_p[:, hv] = wba[:, kh, 0, r]
            wba_p[:, 32 + hv] = wba[:, kh, 1, r]
    wba_t = np.ascontiguousarray(wba_p.reshape(FT, 128, 64))
    convw = np.asarray(L0_attn_conv_w, np.float32)
    convw_t = np.ascontiguousarray(
        convw.reshape(64, 128, 4).transpose(1, 0, 2).reshape(128, 256))
    smalls = np.zeros((128, 8), np.float32)
    smalls[0:NV, 0] = np.asarray(L0_attn_dt_bias, np.float32)
    smalls[0:NV, 1] = np.asarray(L0_attn_A_log, np.float32)
    smalls[NV:2 * NV, 0] = smalls[0:NV, 0]
    smalls[NV:2 * NV, 1] = smalls[0:NV, 1]
    smalls[:, 2] = np.asarray(L0_attn_norm_w, np.float32)
    smalls[:, 3] = 1e-5
    smalls[:, 4] = 1e-6
    smalls[:, 5] = 1.0
    smalls[:, 6] = np.log(1.0 / DK)
    smalls[:, 7] = np.log(DK ** -0.5)

    in_maps = []
    for core in range(8):
        b, j = core // 2, core % 2
        in_maps.append({
            "x_fm": np.ascontiguousarray(x[b].T),
            "wq_t": wq_t, "wo_t": wo_t, "wba_t": wba_t,
            "phi_gu": phi_g, "phi_dn": phi_d,
            "cpw_g": cpg, "cpw_u": cpu_, "cpw_d": cpd,
            "convw_in": convw_t, "smalls_in": smalls,
            "halfsel": np.array([[2 * j, 2 * j + 1]], np.int32),
        })
    return in_maps


def kernel(**inputs):
    in_maps = _host_inputs(**inputs)
    nc, _ = build()
    res = run_bass_kernel_spmd(nc, in_maps, core_ids=list(range(8)))
    y = np.zeros((B, T, HID), np.float32)
    for core in range(8):
        b, j = core // 2, core % 2
        y[b, j * TH:(j + 1) * TH, :] = res.results[core]["y_fm"].T
    return y
